# version 04 (final)
# baseline (speedup 1.0000x reference)
import contextlib
import numpy as np
import ml_dtypes
import concourse.bass as bass
import concourse.mybir as mybir
from concourse.bass_utils import run_bass_kernel_spmd

F32 = mybir.dt.float32
BF16 = mybir.dt.bfloat16
I32 = mybir.dt.int32
AF = mybir.ActivationFunctionType
ALU = mybir.AluOpType
AXX = mybir.AxisListType.X

D = 1024; DI = 2048; DS = 64; NH = 32; HD = 64
CONV = 2176; NDT = 32; XD = CONV + NDT
T = 4096; Q = 128; NCH = 32; SLAB = 512; NSLAB = 8
KD = 8; NFT = 17
EPS = 1e-5

STAGE = 9
_DEV = False


def bc(ap_2d, mid, rep):
    return ap_2d.unsqueeze(2).broadcast_to([ap_2d.shape[0], mid, rep])


def bc2(ap_2d, rep, mid):
    return ap_2d.unsqueeze(1).broadcast_to([ap_2d.shape[0], rep, ap_2d.shape[1]])


def r3(ap_2d, a, b):
    return ap_2d.rearrange("p (a b) -> p a b", a=a)


class Build:
    def __init__(self):
        self.nc = bass.Bass("TRN2", target_bir_lowering=False, debug=False)
        self.es = contextlib.ExitStack()
        self.ops = {e: [] for e in ("sync", "gpsimd", "vector", "scalar", "tensor")}
        self.sems = {}
        for s in ("ld", "sp", "pe", "dve", "act", "pool", "cc"):
            self.sems[s] = [self.es.enter_context(self.nc.semaphore(s)), 0]
        self.dr = {}
        self.sb = {}
        self.w = {}

    def emit(self, eng, fn, waits=(), inc=None, amt=1):
        cum = None
        if inc is not None:
            self.sems[inc][1] += amt
            cum = self.sems[inc][1]
        self.ops[eng].append((fn, [(s, v) for (s, v) in waits if v > 0], inc, amt,
                              cum))
        return cum

    def val(self, s):
        return self.sems[s][1]

    def W(self, name):
        t = self.w.get(name)
        return [t] if t else []

    def setW(self, name, sem, val):
        self.w[name] = (sem, val)

    def dma(self, eng, out, in_, waits=()):
        inc = "ld" if eng == "sync" else "sp"
        return self.emit(eng, lambda e, o=out, i=in_: e.dma_start(out=o, in_=i),
                         waits, inc, 16)

    def mm(self, out, lhsT, rhs, start, stop, waits=()):
        nfree = 1
        for d in rhs.shape[1:]:
            nfree *= d
        if nfree <= 512:
            return self.emit("tensor",
                             lambda e, o=out, l=lhsT, r=rhs, s1=start, s2=stop:
                             e.matmul(o, l, r, start=s1, stop=s2), waits, "pe")
        assert len(rhs.shape) == 2 or (len(rhs.shape) == 3
                                       and rhs.shape[2] * rhs.shape[1] == nfree)
        v = None
        nsp = (nfree + 511) // 512
        for j in range(nsp):
            a, bnd = j * 512, min((j + 1) * 512, nfree)
            if len(rhs.shape) == 3:
                inner = rhs.shape[2]
                ja, jb = a // inner, bnd // inner
                rsl = rhs[:, ja:jb, :]
            else:
                rsl = rhs[:, a:bnd]
            osl = out[:, a:bnd]
            v = self.emit("tensor",
                          lambda e, o=osl, l=lhsT, r=rsl, s1=start, s2=stop:
                          e.matmul(o, l, r, start=s1, stop=s2),
                          waits if j == 0 else (), "pe")
        return v

    def tr(self, out, in_, waits=()):
        np_ = in_.shape[0]
        src = "identb" if in_.dtype == BF16 else "identf"
        idq = self.sb[src][:] if np_ == 128 else self.sb[src][0:np_, 0:np_]
        return self.emit("tensor", lambda e, o=out, i=in_, q=idq:
                         e.transpose(o, i, q), waits, "pe")

    def cp(self, eng, out, in_, waits=()):
        return self.emit(eng, lambda e, o=out, i=in_: e.tensor_copy(o, i), waits,
                         "dve" if eng == "vector" else "pool")

    def tt(self, eng, out, a, b, op, waits=()):
        return self.emit(eng, lambda e, o=out, x=a, y=b, z=op:
                         e.tensor_tensor(o, x, y, z), waits,
                         "dve" if eng == "vector" else "pool")

    def ts(self, eng, out, in_, s1, s2, op0, op1=None, waits=()):
        sem = "dve" if eng == "vector" else "pool"
        if op1 is None:
            return self.emit(eng, lambda e, o=out, i=in_, a=s1, z=op0:
                             e.tensor_scalar(o, i, a, None, z), waits, sem)
        return self.emit(eng, lambda e, o=out, i=in_, a=s1, b=s2, z0=op0, z1=op1:
                         e.tensor_scalar(o, i, a, b, z0, z1), waits, sem)

    def ttb(self, eng, out, in0, bview, op, waits=()):
        a, b2 = bview.shape[1], bview.shape[2]
        return self.tt(eng, r3(out, a, b2), r3(in0, a, b2), bview, op, waits)

    def stt(self, eng, out, in0, sc, in1, op0, op1, waits=()):
        return self.emit(eng, lambda e, o=out, i0=in0, s=sc, i1=in1, z0=op0, z1=op1:
                         e.scalar_tensor_tensor(o, i0, s, i1, z0, z1), waits,
                         "dve" if eng == "vector" else "pool")

    def act(self, out, in_, func, bias=0.0, scale=1.0, waits=()):
        return self.emit("scalar", lambda e, o=out, i=in_, f=func, b=bias, s=scale:
                         e.activation(o, i, f, bias=b, scale=s), waits, "act")

    def red(self, out, in_, op, absv=False, waits=()):
        return self.emit("vector", lambda e, o=out, i=in_, z=op, a=absv:
                         e.tensor_reduce(o, i, AXX, z, apply_absolute_value=a),
                         waits, "dve")

    def mset(self, eng, ap, v, waits=()):
        return self.emit(eng, lambda e, a=ap, w=v: e.memset(a, w), waits,
                         "dve" if eng == "vector" else "pool")

    def bn(self, out, in_, waits=()):
        return self.emit("vector", lambda e, o=out, i=in_: e.bn_stats(o, i),
                         waits, "dve")

    def bnagg(self, out, in_, waits=()):
        return self.emit("vector", lambda e, o=out, i=in_: e.bn_aggr(o, i),
                         waits, "dve")

    def coll(self, cin, cout, waits=()):
        return self.emit("gpsimd", lambda e, a=cin, b=cout: e.collective_compute(
            "AllGather", ALU.bypass, replica_groups=[list(range(8))],
            ins=[a.ap()], outs=[b.ap()]), waits, "cc")

    def finalize(self):
        with self.nc.Block() as block:
            for engname in self.ops:
                dec = getattr(block, engname)

                def make(ops, sems):
                    selfsems = ("dve", "act", "pool")

                    def body(eng):
                        water = {}
                        last_self = None
                        for fn, waits, inc, amt, cum in ops:
                            if inc in selfsems and last_self is not None:
                                sname, sval = last_self
                                if water.get(sname, 0) < sval:
                                    eng.wait_ge(sems[sname][0], sval)
                                    water[sname] = sval
                            for s, v in waits:
                                if water.get(s, 0) < v:
                                    eng.wait_ge(sems[s][0], v)
                                    water[s] = v
                            ins = fn(eng)
                            if inc is not None:
                                ins.then_inc(sems[inc][0], amt)
                                if inc in selfsems:
                                    last_self = (inc, cum)
                    return body

                dec(make(self.ops[engname], self.sems))

    def declare(self):
        nc, dr = self.nc, self.dr

        def din(name, shape, dtype=F32):
            dr[name] = nc.dram_tensor(name, shape, dtype, kind="ExternalInput")

        def dsc(name, shape, dtype=F32):
            kind = "ExternalOutput" if (_DEV and not name.startswith("cc")) \
                else "Internal"
            dr[name] = nc.dram_tensor(name, shape, dtype, kind=kind)

        din("x", [T + 3, D])
        dr["out"] = nc.dram_tensor("out", [T, D], F32, kind="ExternalOutput")
        for i in range(2):
            din(f"wtx{i}", [KD, 128, XD])
            din(f"wtz{i}", [KD, 128, DI])
            din(f"wo{i}", [16, 128, D])
            din(f"cw{i}", [128, NFT * 4])
            din(f"cb{i}", [128, NFT])
            din(f"lnwb{i}", [128, D]); din(f"lnbb{i}", [128, D])
            din(f"dtbb{i}", [128, NDT]); din(f"alogb{i}", [128, NDT])
            din(f"dbb{i}", [128, NDT]); din(f"rmsw{i}", [128, 16])
        din("wgt", [KD, 128, 4096]); din("wut", [KD, 128, 4096]); din("wdt", [32, 128, D])
        din("wg_sl", [128, 4096]); din("wu_sl", [128, 4096]); din("wd_sl", [128, 4096])
        din("identb", [128, 128], BF16); din("triub", [128, 128], BF16)
        din("onesb", [128, 128], BF16); din("triuf", [128, 128], F32)
        din("onescol", [128, 1]); din("onesrow", [1, 128])
        din("identf", [128, 128], F32)
        din("emask", [3, 1]); din("pmask_s", [64, 8]); din("pmask_e", [3, 8])

        dsc("h1", [T, D]); dsc("h2", [T, D])
        for i in range(2):
            dsc(f"ufm{i}", [KD, 128, T], BF16)
            dsc(f"xhtm{i}", [T, DI], BF16)
            dsc(f"btm{i}", [T, DS], BF16)
            dsc(f"bfm{i}", [DS, T], BF16)
            dsc(f"cfm{i}", [DS, T], BF16)
            dsc(f"zsil{i}", [T, DI], BF16)
        dsc("gu_raw", [32, 128, T], BF16)
        dsc("wgq", [KD, 128, 4096], BF16); dsc("wuq", [KD, 128, 4096], BF16)
        dsc("wdq", [32, 128, D], BF16)
        dsc("ccs_in", [DS, NH * HD]); dsc("ccs_out", [8 * DS, NH * HD])
        dsc("cce_in", [3, D]); dsc("cce_out", [24, D])
        dsc("ccw_in", [1, 4]); dsc("ccw_out", [8, 4])
        if _DEV:
            dsc("dbg_dts", [2, 128, NCH * NDT])
            dsc("dbg_gcs", [2, 128, NCH * NDT])
            dsc("dbg_sin", [2, DS, NH * HD])
            dsc("dbg_yg", [2, 128, DI])
            dsc("dbg_mv", [2, 128, NCH * 2])
            dsc("dbg_ygall", [2, NCH, 128, DI])
            dsc("dbg_ygi", [2, NCH, 128, DI])
            dsc("dbg_srun", [2, NCH, DS, NH * HD])
            dsc("dbg_xq", [NCH, 128, D], BF16)
            dsc("dbg_sl", [2, 128, NCH * 2])

        E = self.es.enter_context
        sbd = [
            ("identb", [128, 128], BF16), ("triub", [128, 128], BF16),
            ("onesb", [128, 128], BF16), ("triuf", [128, 128], F32),
            ("onescol", [128, 1], F32), ("onesrow", [1, 128], F32),
            ("identf", [128, 128], F32),
            ("emask", [3, 1], F32), ("pmask_s", [64, 8], F32), ("pmask_e", [3, 8], F32),
            ("dtbb", [128, NDT], F32), ("abb", [128, 2, NDT], F32),
            ("dbb", [128, NDT], F32), ("rmsw", [128, 16], F32),
            ("cw", [128, NFT * 4], F32), ("cb", [128, NFT], F32),
            ("stats0", [128, NCH, 2], F32), ("statsL", [128, NCH, 2], F32),
            ("mv", [128, NCH, 2], F32), ("sq", [128, 12], F32),
            ("hres", [128, 2, D], F32),
            ("psc", [128, 8], F32), ("psc2", [128, 8], F32), ("swb", [128, 4], F32),
            ("cvals", [128, 2], F32),
            ("ccw", [8, 8], F32),
            ("warena", [128, 32768], BF16),
            ("wstage", [128, 1, 4352], F32),
            ("lnwb", [128, D], BF16), ("lnbb", [128, D], BF16),
            ("ufm", [128, KD, SLAB], BF16),
            ("ufmh", [128, KD, 4], BF16),
            ("utm", [128, D], BF16),
            ("xhtm", [128, 2, DI], BF16),
            ("btm", [128, 2, DS], BF16),
            ("bfm", [64, 2, Q], BF16), ("cfm", [64, 2, Q], BF16),
            ("dts", [128, NCH, NDT], F32),
            ("nalck", [128, NCH, NDT], BF16), ("alckb", [128, NCH, NDT], BF16),
            ("gcs", [128, NCH, NDT], F32), ("goff", [128, NCH + 1, NDT], F32),
            ("ltot", [128, NCH, NDT], F32),
            ("ahl", [128, 2, NDT], BF16),
            ("tt32", [128, NDT], F32), ("ebuf", [128, NDT], F32),
            ("dtw", [128, NDT], BF16),
            ("sirun", [64, NH * HD], F32), ("sirunb", [64, NH * HD], BF16),
            ("sm", [128, Q], BF16),
        ]
        for n, s, t in sbd:
            self.sb[n] = E(self.nc.sbuf_tensor("s_" + n, s, t))
        ma = E(self.nc.sbuf_tensor("s_marena", [128, 21888], BF16))
        self.sb["marena"] = ma
        A = self.sb

        def mav(bytoff, nbytes, dtype, shape=None):
            e0 = bytoff // 2
            nel = nbytes // 2
            if shape is not None:
                esz = 2 if dtype == BF16 else 4
                nel = shape[0] * shape[1] * esz // 2
            ap = ma[:, e0:e0 + nel]
            if dtype != BF16:
                ap = ap.bitcast(dtype)
            if shape is not None:
                ap = ap.rearrange("p (a b) -> p a b", a=shape[0])
            return ap

        A["xbc"] = mav(0, 0, BF16, (NFT, SLAB + 3))
        A["xtail"] = mav(17536, 0, BF16, (NFT, 3))
        A["cacc"] = mav(17664, 4096, F32, (2, SLAB))
        A["spf"] = mav(0, 8192, F32)
        A["ms"] = mav(0, 8192, BF16)
        A["xdt"] = mav(8192, 4096, BF16)
        A["xsc"] = mav(12288, 4096, BF16)
        A["zst"] = mav(16384, 4096, BF16)
        A["dclq"] = mav(20480, 4096, F32)
        A["msrhs"] = mav(24576, 2048, BF16)
        A["yasm"] = mav(26624, 8192, F32)
        A["ygn"] = mav(34816, 4096, BF16)
        A["ygf"] = mav(38912, 4096, BF16, (16, 128))
        A["edg"] = mav(0, 4096, F32)[0:3, :]
        A["uh"] = mav(4096, 4096, F32)[0:3, :]
        A["uhb"] = mav(8192, 2048, BF16)[0:3, :]
        self.fv = {}
        self.fv["fpart"] = mav(0, 8192, BF16)
        self.fv["fsb"] = mav(0, 8192, BF16, (32, 128))
        self.fv["fsc"] = mav(8192, 8192, BF16)
        A["fsi"] = mav(16384, 4096, I32)
        A["fg"] = mav(20480, 2048, BF16, (2, SLAB))
        A["fgu"] = mav(16384, 1024, BF16)
        self.fwt5 = mav(22528, 8192, BF16).rearrange(
            "p (b g k c) -> p b g k c", b=2, g=2, k=KD)
        A["xnf"] = mav(30720, 4096, F32)
        A["gout"] = mav(34816, 4096, F32)
        A["h2t"] = mav(38912, 4096, F32)
        A["sclb"] = mav(43008, 512, F32)
        A["fscale"] = mav(43520, 256, F32, (NCH, 2))

        self.pY = E(self.nc.psum_tensor("pY", [128, 2048], F32))
        self.pZ = E(self.nc.psum_tensor("pZ", [128, 1024], F32))
        self.ptr = E(self.nc.psum_tensor("ptr", [128, 1024], BF16))
        self.pA = E(self.nc.psum_tensor("pA", [128, 512], F32))

        self.wtx_v = A["warena"][:, 0:KD * XD].rearrange("p (k c) -> p k c", k=KD)
        self.wtz_v = A["warena"][:, 0:KD * DI].rearrange("p (k c) -> p k c", k=KD)
        self.wo_v = A["warena"][:, KD * DI:KD * DI + 16 * D].rearrange(
            "p (k c) -> p k c", k=16)
        self.xq_v = A["warena"][:, 0:KD * 4096].rearrange("p (k c) -> p k c", k=KD)
        self.wd_v = A["warena"][:, 0:32 * D].rearrange("p (k c) -> p k c", k=32)

    def phase0(self):
        C = self.sb
        for nm in ("identb", "triub", "onesb", "triuf", "onescol", "onesrow",
                   "identf", "emask", "pmask_s", "pmask_e"):
            self.dma("sync", C[nm][:], self.dr[nm].ap())
        for i in range(2):
            self.dma("sync", C["abb"][:, i, :], self.dr[f"alogb{i}"].ap())
        self.vconsts = self.val("ld")
        self.mset("vector", C["cvals"][:, 0:1], EPS)
        self.vceps = self.val("dve")
        abf = C["abb"][:].rearrange("p a b -> p (a b)")
        va = self.act(abf, abf, AF.Exp, waits=[("ld", self.vconsts)])
        self.vA = self.ts("vector", abf, abf, -1.0, None, ALU.mult,
                          waits=[("act", va)])

        for j, nm in enumerate(("wg_sl", "wu_sl", "wd_sl")):
            v = self.dma("sync", C["xnf"][:], self.dr[nm].ap()[:, 0:1024],
                         self.W("slred"))
            va1 = self.red(C["psc"][:, 0:1], C["xnf"][:], ALU.add, True, [("ld", v)])
            v = self.dma("sync", C["gout"][:], self.dr[nm].ap()[:, 1024:2048],
                         self.W("slred"))
            va2 = self.red(C["psc"][:, 1:2], C["gout"][:], ALU.add, True, [("ld", v)])
            v = self.dma("sync", C["xnf"][:], self.dr[nm].ap()[:, 2048:3072],
                         [("dve", va1)])
            va3 = self.red(C["psc"][:, 2:3], C["xnf"][:], ALU.add, True, [("ld", v)])
            v = self.dma("sync", C["gout"][:], self.dr[nm].ap()[:, 3072:4096],
                         [("dve", va2)])
            va4 = self.red(C["psc"][:, 3:4], C["gout"][:], ALU.add, True, [("ld", v)])
            v1 = self.tt("vector", C["psc2"][:, 4:5], C["psc"][:, 0:1], C["psc"][:, 1:2],
                         ALU.add, [("dve", va4)])
            v1 = self.tt("vector", C["psc2"][:, 5:6], C["psc"][:, 2:3], C["psc"][:, 3:4],
                         ALU.add)
            v1 = self.tt("vector", C["psc"][:, 4 + j:5 + j], C["psc2"][:, 4:5],
                         C["psc2"][:, 5:6], ALU.add)
            vm = self.mm(self.pA[0:1, j:j + 1], C["psc"][:, 4 + j:5 + j],
                         C["onescol"][:], True, True,
                         [("dve", v1), ("ld", self.vconsts)])
            self.setW("slred", "dve", v1)
        vps = self.cp("vector", C["psc2"][0:1, 0:4], self.pA[0:1, 0:4],
                      [("pe", self.val("pe"))])
        vdm = self.dma("gpsimd", self.dr["ccw_in"].ap(), C["psc2"][0:1, 0:4],
                       [("dve", vps)])
        self.vcc0 = self.coll(self.dr["ccw_in"], self.dr["ccw_out"], [("sp", vdm)])
        self.setW("hres0", "dve", 0)
        self.setW("hres1", "dve", 0)

    def ffn_scales(self):
        C = self.sb
        vg = self.dma("sync", C["ccw"][0:8, 0:4], self.dr["ccw_out"].ap(),
                      [("cc", self.vcc0)])
        vm = self.mm(self.pA[0:4, 4:5], C["ccw"][0:8, 0:4], C["onescol"][0:8, :],
                     True, True, [("ld", vg)])
        vq = self.cp("vector", C["psc"][0:4, 7:8], self.pA[0:4, 4:5], [("pe", vm)])
        vq = self.ts("vector", C["psc"][0:4, 7:8], C["psc"][0:4, 7:8],
                     1.0 / (4096.0 * 1024.0), 1e-5, ALU.mult, ALU.max,
                     waits=[("dve", vq)])
        vq = self.emit("vector", lambda e: e.reciprocal(C["psc"][0:4, 7:8],
                       C["psc"][0:4, 7:8]), [("dve", vq)], "dve")
        vtq = self.tr(self.pA[0:1, 16:20], C["psc"][0:4, 7:8], [("dve", vq)])
        vq = self.cp("vector", C["psc2"][0:1, 0:4], self.pA[0:1, 16:20],
                     [("pe", vtq)])
        for j in range(3):
            vm = self.mm(self.pZ[:, j:j + 1], C["onesrow"][:],
                         C["psc2"][0:1, j:j + 1], True, True, [("dve", vq)])
        self.vswb = self.cp("vector", C["swb"][:, 0:3], self.pZ[:, 0:3],
                            [("pe", vm)])

    def load_wtx(self, i):
        C = self.sb
        v = 0
        for k in range(KD):
            vld = self.dma("sync", C["wstage"][:, 0, 0:XD], self.dr[f"wtx{i}"].ap()[k],
                           self.W("wst0") + self.W("arena"))
            v = self.cp("vector", self.wtx_v[:, k, :], C["wstage"][:, 0, 0:XD],
                        [("ld", vld)])
            self.setW("wst0", "dve", v)
        return v

    def load_wtz_wo(self, i):
        C = self.sb
        v = 0
        for k in range(KD):
            vld = self.dma("sync", C["wstage"][:, 0, 0:DI], self.dr[f"wtz{i}"].ap()[k],
                           self.W("wst0") + self.W("arena"))
            v = self.cp("vector", self.wtz_v[:, k, :], C["wstage"][:, 0, 0:DI],
                        [("ld", vld)])
            self.setW("wst0", "dve", v)
        for k in range(16):
            vld = self.dma("sync", C["wstage"][:, 0, 0:D], self.dr[f"wo{i}"].ap()[k],
                           self.W("wst0") + self.W("arena") + [("ld", self.vrmsw)])
            v = self.ts("vector", self.wo_v[:, k, :], C["wstage"][:, 0, 0:D],
                        C["rmsw"][:, k:k + 1], None, ALU.mult, None, [("ld", vld)])
            self.setW("wst0", "dve", v)
        return v

    def layer(self, i):
        C = self.sb
        vpe0 = self.val("pe")
        vdve0 = self.val("dve")
        vact0 = self.val("act")
        for bk in range(4):
            self.setW(f"pYb{bk}", "dve", vdve0)
        self.setW("pY", "dve", vdve0)
        self.setW("pZ", "dve", vdve0)
        self.setW("pA", "dve", vdve0)
        self.setW("edg_w", "pe", vpe0)
        self.w["edg_w2"] = ("act", vact0)
        self.w["edg_w3"] = ("dve", vdve0)
        if i == 1:
            self.setW("hres0", "sp", self.vh_done)
            self.setW("hres1", "sp", self.vh_done)
        hsrc = (self.dr["x"] if i == 0 else self.dr["h1"]).ap()
        hdst = (self.dr["h1"] if i == 0 else self.dr["h2"]).ap()
        roff = 3 if i == 0 else 0

        vpool0 = self.val("pool")
        w0 = [("pe", vpe0), ("dve", vdve0), ("act", vact0), ("pool", vpool0)]
        self.vrmsw = self.dma("sync", C["rmsw"][:], self.dr[f"rmsw{i}"].ap(), w0)
        vcw = self.dma("sync", C["cw"][:], self.dr[f"cw{i}"].ap(), w0)
        vcb = self.dma("sync", C["cb"][:], self.dr[f"cb{i}"].ap(), w0)
        vdt = self.dma("sync", C["dtbb"][:], self.dr[f"dtbb{i}"].ap(), w0)
        vdb = self.dma("sync", C["dbb"][:], self.dr[f"dbb{i}"].ap(), w0)
        self.vdbb = vdb
        vld = self.dma("sync", C["wstage"][:, 0, 0:D], self.dr[f"lnwb{i}"].ap(),
                       self.W("wst0"))
        vlw = self.cp("vector", C["lnwb"][:], C["wstage"][:, 0, 0:D], [("ld", vld)])
        self.setW("wst0", "dve", vlw)
        vld = self.dma("sync", C["wstage"][:, 0, 0:D], self.dr[f"lnbb{i}"].ap(),
                       self.W("wst0"))
        vlb = self.cp("vector", C["lnbb"][:], C["wstage"][:, 0, 0:D], [("ld", vld)])
        self.setW("wst0", "dve", vlb)
        vwtx = self.load_wtx(i)
        vlc = self.val("ld")

        vst = 0
        for c in range(NCH):
            bb = c % 2
            vld = self.dma("sync", C["hres"][:, bb, :],
                           hsrc[roff + c * Q: roff + (c + 1) * Q, :],
                           self.W(f"hres{bb}"))
            self.bn(C["sq"][:, 0:6], C["hres"][:, bb, 0:512], [("ld", vld)])
            self.bn(C["sq"][:, 6:12], C["hres"][:, bb, 512:1024])
            vst = self.bnagg(C["mv"][:, c, :], C["sq"][:])
            self.setW(f"hres{bb}", "dve", vst)
        mvf = C["mv"][:].rearrange("p a b -> p (a b)")
        mean_v = mvf[:, 0::2]
        var_v = mvf[:, 1::2]
        stL = C["statsL"][:].rearrange("p a b -> p (a b)")
        rstd1 = stL[:, 0::2]
        va = self.act(rstd1, var_v, AF.Ln, bias=self.sb["cvals"][:, 0:1], waits=[("dve", vst)])
        va = self.act(rstd1, rstd1, AF.Exp, scale=-0.5)
        if i == 0:
            st0 = C["stats0"][:].rearrange("p a b -> p (a b)")
            v1 = self.cp("vector", st0[:, 0::2], rstd1, [("act", va)])
            v1 = self.tt("vector", st0[:, 1::2], st0[:, 0::2], mean_v, ALU.mult)
            v1 = self.ts("vector", st0[:, 1::2], st0[:, 1::2], -1.0, None, ALU.mult)
            tq = C["tt32"][:]
            v1 = self.ts("vector", tq, var_v, EPS, None, ALU.add, waits=[("dve", vst)])
            v1 = self.emit("vector", lambda e: e.reciprocal(tq, tq), [("dve", v1)],
                           "dve")
            v1 = self.tt("vector", tq, tq, var_v, ALU.mult)
            va2 = self.act(tq, tq, AF.Ln, bias=self.sb["cvals"][:, 0:1], waits=[("dve", v1)])
            va2 = self.act(tq, tq, AF.Exp, scale=-0.5)
            v1 = self.tt("vector", rstd1, rstd1, tq, ALU.mult,
                         [("act", va2), ("act", va)])
        v1 = self.tt("vector", stL[:, 1::2], rstd1, mean_v, ALU.mult,
                     [("act", va)])
        vstats = self.ts("vector", stL[:, 1::2], stL[:, 1::2], -1.0, None, ALU.mult)
        if _DEV:
            self.dma("gpsimd", self.dr["dbg_mv"].ap()[i], mvf, [("dve", vstats)])
            self.dma("gpsimd", self.dr["dbg_sl"].ap()[i], stL, [("dve", vstats)])

        if i == 0:
            vled = self.dma("sync", C["edg"][:], self.dr["x"].ap()[0:3, :])
            vedge_rdy = [("ld", vled)]
        else:
            ve = self.mset("vector", C["edg"][:], 0.0,
                           self.W("edg_w") + self.W("edg_w2") + self.W("edg_w3"))
            for r in range(8):
                vld = self.dma("sync", C["uh"][:],
                               self.dr["cce_out"].ap()[r * 3:(r + 1) * 3, :],
                               [("cc", self.vcce), ("dve", ve)])
                ve = self.stt("vector", C["edg"][:], C["uh"][:],
                              C["pmask_e"][:, r:r + 1], C["edg"][:], ALU.mult,
                              ALU.add, [("ld", vld)])
            vedge_rdy = [("dve", ve)]
        v1 = self.bn(C["sq"][0:3, 0:6], C["edg"][0:3, 0:512], vedge_rdy)
        v1 = self.bn(C["sq"][0:3, 6:12], C["edg"][0:3, 512:1024])
        v1 = self.bnagg(C["psc"][0:3, 0:2], C["sq"][0:3, :])
        va = self.act(C["psc"][0:3, 2:3], C["psc"][0:3, 1:2], AF.Ln,
                      bias=self.sb["cvals"][0:3, 0:1], waits=[("dve", v1)])
        va = self.act(C["psc"][0:3, 2:3], C["psc"][0:3, 2:3], AF.Exp, scale=-0.5)
        if i == 0:
            v1 = self.ts("vector", C["psc"][0:3, 3:4], C["psc"][0:3, 1:2], EPS, None,
                         ALU.add, waits=[("dve", v1)])
            v1 = self.emit("vector", lambda e: e.reciprocal(C["psc"][0:3, 3:4],
                           C["psc"][0:3, 3:4]), [("dve", v1)], "dve")
            v1 = self.tt("vector", C["psc"][0:3, 3:4], C["psc"][0:3, 3:4],
                         C["psc"][0:3, 1:2], ALU.mult)
            va2 = self.act(C["psc"][0:3, 3:4], C["psc"][0:3, 3:4], AF.Ln,
                           bias=self.sb["cvals"][0:3, 0:1], waits=[("dve", v1)])
            va2 = self.act(C["psc"][0:3, 3:4], C["psc"][0:3, 3:4], AF.Exp, scale=-0.5)
            v1 = self.tt("vector", C["psc"][0:3, 2:3], C["psc"][0:3, 2:3],
                         C["psc"][0:3, 3:4], ALU.mult, [("act", va2), ("act", va)])
        v1 = self.tt("vector", C["psc"][0:3, 4:5], C["psc"][0:3, 2:3],
                     C["psc"][0:3, 0:1], ALU.mult, [("act", va)])
        v1 = self.ts("vector", C["psc"][0:3, 4:5], C["psc"][0:3, 4:5], -1.0, None,
                     ALU.mult)
        vu = self.ts("vector", C["uh"][:], C["edg"][:], C["psc"][0:3, 2:3],
                     C["psc"][0:3, 4:5], ALU.mult, ALU.add, waits=[("dve", v1)])
        vu = self.tt("vector", C["uh"][:], C["uh"][:], C["lnwb"][0:3, :], ALU.mult,
                     [("dve", vlw)])
        vu = self.tt("vector", C["uh"][:], C["uh"][:], C["lnbb"][0:3, :], ALU.add,
                     [("dve", vlb)])
        vu = self.ts("vector", C["uhb"][:], C["uh"][:], C["emask"][:, 0:1], None,
                     ALU.mult, waits=[("ld", self.vconsts)])
        vt = 0
        for k in range(KD):
            vt = self.tr(self.ptr[:, k * 4:k * 4 + 3], C["uhb"][0:3, k * 128:(k + 1) * 128],
                         [("dve", vu)] + self.W("ptr"))
        vhe = self.cp("vector", C["ufmh"][:].rearrange("p k c -> p (k c)"),
                      self.ptr[:, 0:KD * 4], [("pe", vt)])
        self.setW("ptr", "dve", vhe)
        vufmh = vhe

        for f in range(NFT):
            self.setW(f"conv{f}", "dve", 0)
        for s_ in range(NSLAB):
            vslab_u = 0
            for cc in range(4):
                c = s_ * 4 + cc
                bb = c % 2
                vld = self.dma("sync", C["hres"][:, bb, :],
                               hsrc[roff + c * Q: roff + (c + 1) * Q, :],
                               self.W(f"hres{bb}"))
                vuu = self.ts("vector", C["utm"][:], C["hres"][:, bb, :],
                              C["statsL"][:, c, 0:1], C["statsL"][:, c, 1:2],
                              ALU.mult, ALU.add,
                              waits=[("ld", vld), ("dve", vstats)] + self.W("utm"))
                self.setW(f"hres{bb}", "dve", vuu)
                vuu = self.tt("vector", C["utm"][:], C["utm"][:], C["lnwb"][:],
                              ALU.mult, [("dve", vlw)])
                vuu = self.tt("vector", C["utm"][:], C["utm"][:], C["lnbb"][:],
                              ALU.add, [("dve", vlb)])
                vt = 0
                for k in range(KD):
                    vt = self.tr(self.ptr[:, k * 128:(k + 1) * 128],
                                 C["utm"][:, k * 128:(k + 1) * 128],
                                 [("dve", vuu)] + self.W("ptr"))
                self.setW("utm", "pe", vt)
                vslab_u = self.cp("vector", C["ufm"][:, :, cc * 128:(cc + 1) * 128],
                                  self.ptr[:].rearrange("p (k c) -> p k c", k=KD),
                                  [("pe", vt)] + self.W("ufm_mm"))
                self.setW("ptr", "dve", vslab_u)
            vusp = self.dma("gpsimd",
                            self.dr[f"ufm{i}"].ap()[:, :, s_ * SLAB:(s_ + 1) * SLAB]
                            .rearrange("k p c -> p k c"),
                            C["ufm"][:], [("dve", vslab_u)])
            vdtl = 0
            for cc in range(4):
                c = s_ * 4 + cc
                for k in range(KD):
                    vpe = self.mm(self.pA[:, 0:NDT],
                                  C["ufm"][:, k, cc * 128:(cc + 1) * 128],
                                  self.wtx_v[:, k, CONV:CONV + NDT],
                                  k == 0, k == KD - 1,
                                  ([("dve", vslab_u), ("dve", vwtx)]
                                   + self.W("pA")) if k == 0 else [])
                vdtl = self.tt("vector", C["dts"][:, c, :], self.pA[:, 0:NDT],
                               C["dtbb"][:], ALU.add, [("pe", vpe), ("ld", vdt)])
                self.setW("pA", "dve", vdtl)
            self.vdts_raw = vdtl
            self.setW("ufm_mm", "pe", self.val("pe"))
            for f in range(NFT):
                bank = f % 4
                for k in range(KD):
                    vpe = self.mm(self.pY[:, bank * 512:(bank + 1) * 512],
                                  self.wtx_v[:, k, f * 128:(f + 1) * 128],
                                  C["ufm"][:, k, 0:SLAB],
                                  k == 0, k == KD - 1,
                                  ([("dve", vslab_u), ("dve", vwtx)]
                                   + self.W(f"pYb{bank}")) if k == 0 else [])
                if s_ == 0:
                    vcar = self.mset("vector", C["xbc"][:, f, 0:3], 0.0,
                                     self.W(f"conv{f}"))
                else:
                    vcar = self.cp("vector", C["xbc"][:, f, 0:3],
                                   C["xtail"][:, f, :],
                                   self.W(f"conv{f}"))
                vev = self.cp("vector", C["xbc"][:, f, 3:3 + SLAB],
                              self.pY[:, bank * 512:(bank + 1) * 512],
                              [("pe", vpe), ("dve", vcar)])
                self.setW(f"pYb{bank}", "dve", vev)
                vtl = self.cp("vector", C["xtail"][:, f, :],
                              C["xbc"][:, f, SLAB:SLAB + 3])
                self.setW(f"xbc{f}", "dve", vtl)
            if s_ == 0:
                for f in range(NFT):
                    for k in range(KD):
                        vpe = self.mm(self.pA[:, 64:67],
                                      self.wtx_v[:, k, f * 128:(f + 1) * 128],
                                      C["ufmh"][:, k, 0:3], k == 0, k == KD - 1,
                                      ([("dve", vufmh), ("dve", vwtx)]
                                       + self.W("pA")) if k == 0 else [])
                    vha = self.cp("vector", C["xbc"][:, f, 0:3], self.pA[:, 64:67],
                                  [("pe", vpe)] + self.W(f"xbc{f}"))
                    self.setW("pA", "dve", vha)
                    self.setW(f"xbc{f}", "dve", vha)
            self.setW("ufm_mm", "pe", self.val("pe"))
            for f in range(NFT):
                eng = "vector"
                sem = "dve"
                w0 = [self.w[f"xbc{f}"], ("ld", vcw), ("ld", vcb)]
                ab = f % 2
                vc = self.ts(eng, C["cacc"][:, ab, :], C["xbc"][:, f, 0:SLAB],
                             C["cw"][:, 4 * f:4 * f + 1], C["cb"][:, f:f + 1],
                             ALU.mult, ALU.add,
                             waits=w0 + self.W(f"cacc{ab}"))
                for j in (1, 2):
                    vc = self.stt(eng, C["cacc"][:, ab, :], C["xbc"][:, f, j:j + SLAB],
                                  C["cw"][:, 4 * f + j:4 * f + j + 1],
                                  C["cacc"][:, ab, :], ALU.mult, ALU.add)
                vc = self.stt(eng, C["xbc"][:, f, 3:3 + SLAB],
                              C["xbc"][:, f, 3:3 + SLAB],
                              C["cw"][:, 4 * f + 3:4 * f + 4], C["cacc"][:, ab, :],
                              ALU.mult, ALU.add)
                self.setW(f"cacc{ab}", sem, vc)
                self.setW(f"conv{f}", sem, vc)
                self.w[f"conv{f}"] = (sem, vc)
            sem16 = self.w["conv16"]
            vbc = self.act(C["xbc"][:, 16, 3:3 + SLAB], C["xbc"][:, 16, 3:3 + SLAB],
                           AF.Silu, waits=[sem16])
            self.w["conv16"] = ("act", vbc)
            for cc in range(4):
                c = s_ * 4 + cc
                bb = c % 2
                vb1 = self.cp("vector", C["bfm"][:, bb, :],
                              C["xbc"][0:64, 16, 3 + cc * 128:3 + (cc + 1) * 128],
                              [("act", vbc)] + self.W(f"bfm{bb}"))
                vb2 = self.cp("vector", C["cfm"][:, bb, :],
                              C["xbc"][64:128, 16, 3 + cc * 128:3 + (cc + 1) * 128],
                              [("act", vbc)] + self.W(f"cfm{bb}"))
                vtb = self.tr(self.ptr[:, 0:64], C["bfm"][:, bb, :],
                              [("dve", vb1)] + self.W("ptr"))
                vb3 = self.cp("vector", C["btm"][:, bb, :], self.ptr[:, 0:64],
                              [("pe", vtb)] + self.W(f"btm{bb}"))
                self.setW("ptr", "dve", vb3)
                vd1 = self.dma("gpsimd", self.dr[f"bfm{i}"].ap()[:, c * Q:(c + 1) * Q],
                               C["bfm"][:, bb, :], [("dve", vb1)])
                self.setW(f"bfm{bb}", "sp", vd1)
                vd2 = self.dma("gpsimd", self.dr[f"cfm{i}"].ap()[:, c * Q:(c + 1) * Q],
                               C["cfm"][:, bb, :], [("dve", vb2)])
                self.setW(f"cfm{bb}", "sp", vd2)
                vd3 = self.dma("gpsimd", self.dr[f"btm{i}"].ap()[c * Q:(c + 1) * Q, :],
                               C["btm"][:, bb, :], [("dve", vb3)])
                self.setW(f"btm{bb}", "sp", vd3)
            for cc in range(4):
                c = s_ * 4 + cc
                bb = c % 2
                for half in range(2):
                    vt = 0
                    for ff in range(8):
                        f = half * 8 + ff
                        vt = self.tr(self.ptr[:, ff * 128:(ff + 1) * 128],
                                     C["xbc"][:, f, 3 + cc * 128:3 + (cc + 1) * 128],
                                     [self.w[f"conv{f}"]] + self.W("ptr"))
                    vsl = self.act(C["xhtm"][:, bb, half * 1024:(half + 1) * 1024],
                                   self.ptr[:], AF.Silu,
                                   waits=[("pe", vt)] + self.W(f"xhtm{bb}"))
                    self.setW("ptr", "act", vsl)
                vxs = self.dma("gpsimd", self.dr[f"xhtm{i}"].ap()[c * Q:(c + 1) * Q, :],
                               C["xhtm"][:, bb, :], [("act", vsl)])
                self.setW(f"xhtm{bb}", "sp", vxs)
        self.vxh_spill = self.val("sp")

        dts_f = C["dts"][:].rearrange("p a b -> p (a b)")
        spf = C["spf"]
        v1 = self.act(spf[:, 0:1024], dts_f, AF.Abs,
                      waits=[("dve", self.vdts_raw), ("pe", self.val("pe"))])
        v1 = self.act(spf[:, 0:1024], spf[:, 0:1024], AF.Exp, scale=-1.0)
        v1 = self.act(spf[:, 0:1024], spf[:, 0:1024], AF.Ln, bias=1.0)
        v2 = self.emit("vector", lambda e: e.tensor_relu(spf[:, 1024:2048], dts_f),
                       [("dve", self.vdts_raw), ("pe", self.val("pe"))], "dve")
        vdts = self.tt("vector", dts_f, spf[:, 0:1024], spf[:, 1024:2048], ALU.add,
                       [("act", v1), ("dve", v2)])
        self.vdts = vdts
        alpha_f = spf[:, 0:1024]
        vuse = self.ttb("vector", alpha_f, dts_f,
                        bc2(C["abb"][:, i, :], NCH, NDT), ALU.mult,
                        [("dve", vdts), ("dve", self.vA)])
        v1 = self.cp("vector", C["alckb"][:].rearrange("p a b -> p (a b)"), alpha_f)
        valck = self.ts("vector", C["nalck"][:].rearrange("p a b -> p (a b)"),
                        alpha_f, -1.0, None, ALU.mult)
        self.valck = valck
        vcs = 0
        for c in range(NCH):
            v1 = self.cp("vector", C["ahl"][:, 0, :], alpha_f[:, c * NDT:(c + 1) * NDT],
                         self.W("ahl"))
            v1 = self.tt("vector", C["tt32"][:], alpha_f[:, c * NDT:(c + 1) * NDT],
                         C["ahl"][:, 0, :], ALU.subtract)
            v1 = self.cp("vector", C["ahl"][:, 1, :], C["tt32"][:])
            vm1 = self.mm(self.pA[:, 0:NDT], C["triub"][:], C["ahl"][:, 0, :],
                          True, False, [("dve", v1)] + self.W("pA"))
            vm1 = self.mm(self.pA[:, 0:NDT], C["triub"][:], C["ahl"][:, 1, :],
                          False, True)
            vm2 = self.mm(self.pA[:, NDT:2 * NDT], C["onesb"][:], C["ahl"][:, 0, :],
                          True, False)
            vm2 = self.mm(self.pA[:, NDT:2 * NDT], C["onesb"][:], C["ahl"][:, 1, :],
                          False, True)
            self.setW("ahl", "pe", vm2)
            vcs = self.cp("vector", C["gcs"][:, c, :], self.pA[:, 0:NDT],
                          [("pe", vm2), ("dve", vcs)])
            vcs = self.cp("vector", C["ltot"][:, c, :], self.pA[:, NDT:2 * NDT])
            self.setW("pA", "dve", vcs)
        vcs = self.mset("vector", C["goff"][:, NCH, :], 0.0, [("dve", vcs)])
        for c in range(NCH - 1, -1, -1):
            vcs = self.tt("vector", C["goff"][:, c, :], C["goff"][:, c + 1, :],
                          C["ltot"][:, c, :], ALU.add)
        self.vgcs = vcs
        if _DEV:
            self.dma("gpsimd", self.dr["dbg_dts"].ap()[i], dts_f, [("dve", vcs)])
            self.dma("gpsimd", self.dr["dbg_gcs"].ap()[i],
                     C["gcs"][:].rearrange("p a b -> p (a b)"), [("dve", vcs)])

        gcs_f = C["gcs"][:].rearrange("p a b -> p (a b)")
        goff_f = C["goff"][:, 0:NCH, :].rearrange("p a b -> p (a b)")
        vwg = self.tt("vector", spf[:, 1024:2048], gcs_f, goff_f, ALU.subtract,
                      [("dve", vcs)])
        vwg = self.act(spf[:, 1024:2048], spf[:, 1024:2048], AF.Exp, scale=-1.0,
                       waits=[("dve", vwg)])
        vwg = self.tt("vector", C["dtw"][:] if False else spf[:, 0:1024],
                      spf[:, 1024:2048], dts_f, ALU.mult, [("act", vwg)])
        self.setW("xh0", "pool", 0); self.setW("xh1", "pool", 0)
        self.setW("btm0", "pe", 0); self.setW("btm1", "pe", 0)
        vsmm = 0
        for c in range(NCH):
            bb = c % 2
            vld1 = self.dma("sync", C["xhtm"][:, bb, :],
                            self.dr[f"xhtm{i}"].ap()[c * Q:(c + 1) * Q, :],
                            self.W(f"xh{bb}") + [("sp", self.vxh_spill)])
            vld2 = self.dma("sync", C["btm"][:, bb, :],
                            self.dr[f"btm{i}"].ap()[c * Q:(c + 1) * Q, :],
                            self.W(f"btm{bb}"))
            vwb = self.cp("vector", C["dtw"][:], spf[:, c * NDT:(c + 1) * NDT],
                          [("dve", vwg)] + self.W("dtw"))
            vxw = self.ttb("gpsimd", C["xsc"][:], C["xhtm"][:, bb, :],
                           bc(C["dtw"][:], NDT, HD), ALU.mult,
                           [("ld", vld1), ("dve", vwb)] + self.W("xsc"))
            self.setW(f"xh{bb}", "pool", vxw)
            self.setW("dtw", "pool", vxw)
            vsmm = self.mm(self.pY[0:64, 0:1024], C["btm"][:, bb, :],
                           C["xsc"][:, 0:1024], c == 0, False,
                           [("pool", vxw), ("ld", vld2)] +
                           (sum((self.W(f"pYb{bk}") for bk in range(4)), [])
                            if c == 0 else []))
            vsmm = self.mm(self.pY[0:64, 1024:2048], C["btm"][:, bb, :],
                           C["xsc"][:, 1024:2048], c == 0, c == NCH - 1)
            self.setW("xsc", "pe", vsmm)
            self.setW(f"btm{bb}", "pe", vsmm)
        vev = self.cp("vector", C["sirun"][:], self.pY[0:64, 0:2048], [("pe", vsmm)])
        vdm = self.dma("gpsimd", self.dr["ccs_in"].ap(), C["sirun"][:], [("dve", vev)])
        vccs = self.coll(self.dr["ccs_in"], self.dr["ccs_out"], [("sp", vdm)])
        self.setW("arena", "pe", vsmm)
        self.setW("pY", "dve", vev)

        self.setW("ms2", "dve", self.val("dve"))
        self.setW("dclq", "pool", self.val("pool"))
        vw2 = self.load_wtz_wo(i)

        vinit = self.mset("vector", C["sirun"][:], 0.0, [("sp", vdm)])
        for r in range(8):
            vldg = self.dma("sync", C["yasm"][0:64, 0:2048],
                            self.dr["ccs_out"].ap()[r * 64:(r + 1) * 64, :],
                            [("cc", vccs), ("dve", vinit)])
            vinit = self.stt("vector", C["sirun"][:], C["yasm"][0:64, 0:2048],
                             C["pmask_s"][:, r:r + 1], C["sirun"][:], ALU.mult,
                             ALU.add, [("ld", vldg)])
        vsirb = self.cp("vector", C["sirunb"][:], C["sirun"][:], [("dve", vinit)])
        if _DEV:
            self.dma("gpsimd", self.dr["dbg_sin"].ap()[i], C["sirun"][:],
                     [("dve", vinit)])

        vpe_1b = self.val("pe")
        self.setW("uz0", "pe", vpe_1b); self.setW("uz1", "pe", vpe_1b)
        uz = C["ufm"][:].rearrange("p k c -> p (k c)")[:, 0:2 * KD * Q].rearrange(
            "p (b k c) -> p b k c", b=2, k=KD)
        vzs = 0
        for c in range(NCH):
            bb = c % 2
            vldu = self.dma("sync", uz[:, bb, :, :],
                            self.dr[f"ufm{i}"].ap()[:, :, c * Q:(c + 1) * Q].rearrange(
                                "k p c -> p k c"),
                            self.W(f"uz{bb}"))
            for half in range(2):
                for k in range(KD):
                    vpe = self.mm(self.pZ[:, 0:1024], uz[:, bb, k, :],
                                  self.wtz_v[:, k, half * 1024:(half + 1) * 1024],
                                  k == 0, k == KD - 1,
                                  ([("ld", vldu), ("dve", vw2)] + self.W("pZ"))
                                  if k == 0 else [])
                vsz = self.act(C["zst"][:, half * 1024:(half + 1) * 1024],
                               self.pZ[:, 0:1024], AF.Silu,
                               waits=[("pe", vpe)] + self.W("zst"))
                self.setW("pZ", "act", vsz)
            self.setW(f"uz{bb}", "pe", vpe)
            vzs = self.dma("gpsimd", self.dr[f"zsil{i}"].ap()[c * Q:(c + 1) * Q, :],
                           C["zst"][:], [("act", vsz)])
            self.setW("zst", "sp", vzs)
        self.setW("ufm_mm", "pe", self.val("pe"))
        self.setW("xh0", "pool", 0); self.setW("xh1", "pool", 0)
        self.setW("b2_0", "pe", 0); self.setW("b2_1", "pe", 0)
        vh_out = 0
        for c in range(NCH):
            bb = c % 2
            vldx = self.dma("sync", C["xhtm"][:, bb, :],
                            self.dr[f"xhtm{i}"].ap()[c * Q:(c + 1) * Q, :],
                            self.W(f"xh{bb}"))
            vldb = self.dma("sync", C["btm"][:, bb, :],
                            self.dr[f"btm{i}"].ap()[c * Q:(c + 1) * Q, :],
                            self.W(f"btm{bb}"))
            vldbf = self.dma("sync", C["bfm"][:, bb, :],
                             self.dr[f"bfm{i}"].ap()[:, c * Q:(c + 1) * Q],
                             self.W(f"bfm{bb}"))
            vldcf = self.dma("sync", C["cfm"][:, bb, :],
                             self.dr[f"cfm{i}"].ap()[:, c * Q:(c + 1) * Q],
                             self.W(f"cfm{bb}"))
            vldz = self.dma("sync", C["zst"][:],
                            self.dr[f"zsil{i}"].ap()[c * Q:(c + 1) * Q, :],
                            self.W("zst") + [("sp", vzs)])
            vldh = self.dma("sync", C["hres"][:, bb, :],
                            hsrc[roff + c * Q: roff + (c + 1) * Q, :],
                            self.W(f"hres{bb}"))
            vg = self.mm(self.pY[:, 0:1024], C["cfm"][:, bb, :],
                         C["sirunb"][0:64, 0:1024], True, True,
                         [("ld", vldcf), ("dve", vsirb)] + self.W("pY"))
            vg = self.mm(self.pY[:, 1024:2048], C["cfm"][:, bb, :],
                         C["sirunb"][0:64, 1024:2048], True, True)
            va = self.act(C["ebuf"][:], C["gcs"][:, c, :], AF.Exp,
                          waits=[("dve", self.vgcs)])
            vya = self.ttb("vector", C["yasm"][:], self.pY[:, 0:2048],
                           bc(C["ebuf"][:], NDT, HD), ALU.mult,
                           [("pe", vg), ("act", va)] + self.W("yasm"))
            vxd = self.ttb("gpsimd", C["xsc"][:], C["xhtm"][:, bb, :],
                           bc(C["dbb"][:], NDT, HD), ALU.mult,
                           [("ld", vldx), ("ld", self.vdbb)] + self.W("xsc"))
            vya = self.tt("vector", C["yasm"][:], C["yasm"][:], C["xsc"][:], ALU.add,
                          [("pool", vxd)])
            self.setW("xsc", "dve", vya)
            self.setW("pY", "dve", vya)
            vsc = self.mm(self.pA[:, 0:128], C["bfm"][:, bb, :], C["cfm"][:, bb, :],
                          True, True,
                          [("ld", vldbf), ("ld", vldcf)] + self.W("pA"))
            vsm = self.tt("vector", C["sm"][:], self.pA[:, 0:128], C["triuf"][:],
                          ALU.mult, [("pe", vsc), ("ld", self.vconsts)] + self.W("sm"))
            self.setW("pA", "dve", vsm)
            vmsk = 0
            for qtr in range(4):
                hs = qtr * 8
                vrhs = self.tt("vector", r3(C["msrhs"][:], 8, 128),
                               bc(C["nalck"][:, c, hs:hs + 8], 8, 128),
                               bc2(C["triub"][:], 8, 128), ALU.mult,
                               [("dve", valck)] + self.W("msrhs"))
                vq1 = self.mm(self.pZ[:, 0:1024], C["triub"][:],
                              bc(C["alckb"][:, c, hs:hs + 8], 8, 128), True, False,
                              [("dve", v1)] + self.W("pZ"))
                vq1 = self.mm(self.pZ[:, 0:1024], C["onesb"][:], C["msrhs"][:],
                              False, True, [("dve", vrhs)])
                self.setW("msrhs", "pe", vq1)
                vr = self.act(C["dclq"][:], self.pZ[:, 0:1024], AF.Relu,
                              waits=[("pe", vq1)] + self.W("dclq"))
                self.setW("pZ", "act", vr)
                vr = self.act(C["ms"][:, qtr * 1024:(qtr + 1) * 1024], C["dclq"][:],
                              AF.Exp, scale=-1.0, waits=self.W("ms2"))
                self.setW("dclq", "act", vr)
                vmsk = self.ttb("vector", C["ms"][:, qtr * 1024:(qtr + 1) * 1024],
                                C["ms"][:, qtr * 1024:(qtr + 1) * 1024],
                                bc2(C["sm"][:], 8, 128), ALU.mult,
                                [("act", vr), ("dve", vsm)])
            self.setW("sm", "dve", vmsk)
            vxt = self.ttb("gpsimd", C["xdt"][:], C["xhtm"][:, bb, :],
                           bc(C["dts"][:, c, :], NDT, HD), ALU.mult,
                           [("ld", vldx), ("dve", self.vdts)] + self.W("xdt"))
            vin = 0
            for h in range(NH):
                vin = self.mm(self.pY[:, h * HD:(h + 1) * HD],
                              C["ms"][:, h * Q:(h + 1) * Q],
                              C["xdt"][:, h * HD:(h + 1) * HD], True, True,
                              ([("dve", vmsk), ("pool", vxt), ("dve", vya)])
                              if h == 0 else [])
            self.setW("ms2", "pe", vin)
            self.setW("xdt", "pe", vin)
            vyg = self.tt("vector", C["yasm"][:], C["yasm"][:], self.pY[:, 0:2048],
                          ALU.add, [("pe", vin)])
            self.setW("pY", "dve", vyg)
            vyg = self.tt("vector", C["yasm"][:], C["yasm"][:], C["zst"][:], ALU.mult,
                          [("ld", vldz)])
            self.setW("zst", "dve", vyg)
            if _DEV:
                vdtap = self.dma("gpsimd", self.dr["dbg_ygall"].ap()[i][c],
                                 C["yasm"][:], [("dve", vyg)])
                self.setW("yasm", "sp", vdtap)
            v1 = self.tt("vector", C["ebuf"][:], C["ltot"][:, c, :],
                         C["gcs"][:, c, :], ALU.subtract, [("dve", self.vgcs)])
            va2 = self.act(C["ebuf"][:], C["ebuf"][:], AF.Exp, waits=[("dve", v1)])
            vad = self.act(C["tt32"][:], C["ltot"][:, c, :], AF.Exp,
                           waits=[("dve", v1)] + self.W("tt32"))
            v1 = self.tt("vector", C["dtw"][:], C["dts"][:, c, :], C["ebuf"][:],
                         ALU.mult, [("act", va2)] + self.W("dtw"))
            vxw = self.ttb("gpsimd", C["xsc"][:], C["xhtm"][:, bb, :],
                           bc(C["dtw"][:], NDT, HD), ALU.mult,
                           [("dve", v1), ("dve", vya)] + self.W("xsc"))
            self.setW("dtw", "pool", vxw)
            self.setW(f"xh{bb}", "pool", vxw)
            vdm1 = self.ttb("gpsimd", C["sirun"][:], C["sirun"][:],
                            bc(C["tt32"][0:64, :], NDT, HD), ALU.mult,
                            [("act", vad), ("pe", vg), ("dve", vsirb)]
                            + self.W("sirun_tap"))
            self.setW("tt32", "pool", vdm1)
            vsl = self.mm(self.pZ[0:64, 0:1024], C["btm"][:, bb, :],
                          C["xsc"][:, 0:1024], True, True,
                          [("pool", vxw), ("ld", vldb)] + self.W("pZ"))
            vch = self.tt("vector", C["sirun"][0:64, 0:1024],
                          C["sirun"][0:64, 0:1024], self.pZ[0:64, 0:1024], ALU.add,
                          [("pe", vsl), ("pool", vdm1)])
            vsl = self.mm(self.pZ[0:64, 0:1024], C["btm"][:, bb, :],
                          C["xsc"][:, 1024:2048], True, True, [("dve", vch)])
            self.setW(f"btm{bb}", "pe", vsl)
            self.setW("xsc", "pe", vsl)
            vch = self.tt("vector", C["sirun"][0:64, 1024:2048],
                          C["sirun"][0:64, 1024:2048], self.pZ[0:64, 0:1024],
                          ALU.add, [("pe", vsl)])
            self.setW("pZ", "dve", vch)
            vsirb = self.cp("vector", C["sirunb"][:], C["sirun"][:],
                            [("dve", vch), ("pe", vg)])
            if _DEV:
                vtap3 = self.dma("gpsimd", self.dr["dbg_srun"].ap()[i][c],
                                 C["sirun"][:], [("dve", vch)])
                self.setW("sirun_tap", "sp", vtap3)
            v1 = self.bn(C["sq"][:, 0:6], C["yasm"][:, 0:512], [("dve", vyg)])
            self.bn(C["sq"][:, 6:12], C["yasm"][:, 512:1024])
            self.bnagg(C["psc"][:, 0:2], C["sq"][:])
            self.bn(C["sq"][:, 0:6], C["yasm"][:, 1024:1536])
            self.bn(C["sq"][:, 6:12], C["yasm"][:, 1536:2048])
            v1 = self.bnagg(C["psc"][:, 2:4], C["sq"][:])
            v1 = self.tt("vector", C["psc"][:, 4:5], C["psc"][:, 0:1],
                         C["psc"][:, 0:1], ALU.mult, [("dve", v1)])
            v1 = self.tt("vector", C["psc"][:, 4:5], C["psc"][:, 4:5],
                         C["psc"][:, 1:2], ALU.add)
            v1 = self.tt("vector", C["psc"][:, 5:6], C["psc"][:, 2:3],
                         C["psc"][:, 2:3], ALU.mult)
            v1 = self.tt("vector", C["psc"][:, 5:6], C["psc"][:, 5:6],
                         C["psc"][:, 3:4], ALU.add)
            v1 = self.tt("vector", C["psc"][:, 4:5], C["psc"][:, 4:5],
                         C["psc"][:, 5:6], ALU.add)
            v1 = self.ts("vector", C["psc"][:, 4:5], C["psc"][:, 4:5], 0.5, None,
                         ALU.mult)
            va3 = self.act(C["psc"][:, 5:6], C["psc"][:, 4:5], AF.Ln, bias=self.sb["cvals"][:, 0:1],
                           waits=[("dve", v1)])
            va3 = self.act(C["psc"][:, 5:6], C["psc"][:, 5:6], AF.Exp, scale=-0.5)
            vygn = self.ts("vector", C["ygn"][:], C["yasm"][:], C["psc"][:, 5:6],
                           None, ALU.mult, waits=[("act", va3)] + self.W("ygn"))
            if _DEV:
                vdt2 = self.dma("gpsimd", self.dr["dbg_ygi"].ap()[i][c],
                                C["ygn"][:].bitcast(F32) if False else C["ygn"][:],
                                [("dve", vygn)])
                self.setW("ygn", "sp", vdt2)
            self.setW("yasm", "dve", vygn)
            for half in range(2):
                vt = 0
                for ff in range(8):
                    f = half * 8 + ff
                    vt = self.tr(self.ptr[:, ff * 128:(ff + 1) * 128],
                                 C["ygn"][:, f * 128:(f + 1) * 128],
                                 [("dve", vygn)] + self.W("ptr"))
                vhf = self.cp("vector", C["ygf"][:, half * 8:(half + 1) * 8, :],
                              self.ptr[:].rearrange("p (k c) -> p k c", k=8),
                              [("pe", vt)] + self.W("ygf"))
                self.setW("ptr", "dve", vhf)
            for half in range(2):
                for kk in range(16):
                    vop = self.mm(self.pZ[:, half * 512:(half + 1) * 512],
                                  C["ygf"][:, kk, :],
                                  self.wo_v[:, kk, half * 512:(half + 1) * 512],
                                  kk == 0, kk == 15,
                                  ([("dve", vhf), ("dve", vw2)] + self.W("pZ"))
                                  if (kk == 0 and half == 0) else [])
            self.setW("ygf", "pe", vop)
            self.setW("ygn", "pe", vop)
            if i == 0:
                vxn = self.ts("vector", C["hres"][:, bb, :], C["hres"][:, bb, :],
                              C["stats0"][:, c, 0:1], C["stats0"][:, c, 1:2],
                              ALU.mult, ALU.add, waits=[("ld", vldh)])
            else:
                vxn = None
            vho = self.tt("vector", C["hres"][:, bb, :], C["hres"][:, bb, :],
                          self.pZ[:, 0:1024], ALU.add,
                          [("pe", vop)] + ([("dve", vxn)] if vxn else [("ld", vldh)]))
            self.setW("pZ", "dve", vho)
            vh_out = self.dma("gpsimd", hdst[c * Q:(c + 1) * Q, :],
                              C["hres"][:, bb, :], [("dve", vho)])
            self.setW(f"hres{bb}", "sp", vh_out)
        self.setW("arena", "pe", self.val("pe"))
        self.setW("ufm_mm", "pe", 0)
        if i == 0:
            vdm = self.dma("gpsimd", self.dr["cce_in"].ap(),
                           self.dr["h1"].ap()[T - 3:T, :], [("sp", vh_out)])
            self.vcce = self.coll(self.dr["cce_in"], self.dr["cce_out"],
                                  [("sp", vdm)])
        self.vh_done = vh_out

    def ffn(self):
        C = self.sb
        fpart = self.fv["fpart"]
        fsbv = self.fv["fsb"]
        fscv = self.fv["fsc"]
        self.ffn_scales()
        vi = self.emit("vector", lambda e: e.reciprocal(C["swb"][:, 3:4],
                       C["swb"][:, 0:1]), [("dve", self.vswb)], "dve")
        vi2 = self.emit("vector", lambda e: e.reciprocal(C["psc2"][:, 0:1],
                        C["swb"][:, 1:2]), [("dve", self.vswb)], "dve")
        vi3 = self.emit("vector", lambda e: e.reciprocal(C["psc2"][:, 1:2],
                        C["swb"][:, 2:3]), [("dve", self.vswb)], "dve")
        vsc = self.tt("vector", C["psc2"][:, 2:3], C["psc2"][:, 0:1],
                      C["psc2"][:, 1:2], ALU.mult, [("dve", vi3)])
        vswc = self.ts("vector", C["psc2"][:, 2:3], C["psc2"][:, 2:3], 1.0 / 127.0,
                       None, ALU.mult)
        for j, (src, dst, nk) in enumerate(
                (("wgt", "wgq", KD), ("wut", "wuq", KD), ("wdt", "wdq", 32))):
            ncols = 4096 if nk == KD else D
            wst = C["wstage"][:].rearrange("p a b -> p (a b)")[:, 0:ncols]
            for k in range(nk):
                vld = self.dma("sync", wst, self.dr[src].ap()[k], self.W("wstq"))
                v1 = self.ts("vector", wst, wst, C["swb"][:, j:j + 1], None, ALU.mult,
                             waits=[("ld", vld), ("dve", self.vswb)])
                v1 = self.ts("vector", wst, wst, 1.0, -1.0, ALU.min, ALU.max)
                vlast = 0
                for hh in range(ncols // 1024):
                    sl = slice(hh * 1024, (hh + 1) * 1024)
                    v2 = self.cp("vector", C["fsi"][:, 0:1024], wst[:, sl])
                    v2 = self.emit("vector", lambda e, o=fscv[:, sl]:
                                   e.tensor_copy(o, C["fsi"][:, 0:1024]),
                                   self.W("fscq"), "dve")
                    vlast = v2
                vd = self.dma("gpsimd", self.dr[dst].ap()[k], fscv[:, 0:ncols],
                              [("dve", vlast)])
                self.setW("fscq", "sp", vd)
                self.setW("wstq", "dve", vlast)
        vwq = self.val("sp")
        vxq = 0
        vpe_mamba = self.val("pe")
        for c in range(NCH):
            bb = c % 2
            vld = self.dma("sync", C["hres"][:, bb, :],
                           self.dr["x"].ap()[3 + c * Q:3 + (c + 1) * Q, :],
                           self.W(f"hres{bb}"))
            vxn = self.ts("vector", C["xnf"][:], C["hres"][:, bb, :],
                          C["stats0"][:, c, 0:1], C["stats0"][:, c, 1:2],
                          ALU.mult, ALU.add, waits=[("ld", vld)] + self.W("xnf"))
            self.setW(f"hres{bb}", "dve", vxn)
            vam = self.red(C["psc"][:, 0:1], C["xnf"][:], ALU.max, True)
            vam = self.ts("vector", C["psc"][:, 0:1], C["psc"][:, 0:1], 1e-5, None,
                          ALU.max)
            vam = self.emit("vector", lambda e: e.reciprocal(C["psc"][:, 1:2],
                            C["psc"][:, 0:1]), [("dve", vam)], "dve")
            vsx = self.ts("vector", C["psc"][:, 2:3], C["psc"][:, 1:2], 127.0,
                          None, ALU.mult)
            visx = self.ts("vector", C["psc"][:, 3:4], C["psc"][:, 0:1], 1.0 / 127.0,
                           None, ALU.mult)
            vq = self.ts("vector", C["gout"][:], C["xnf"][:], C["psc"][:, 2:3],
                         None, ALU.mult, waits=[("dve", vsx)] + self.W("gout"))
            vq = self.cp("vector", C["fsi"][:, 0:1024], C["gout"][:])
            vq = self.cp("vector", C["gout"][:], C["fsi"][:, 0:1024])
            vq = self.ts("vector", C["utm"][:], C["gout"][:], C["psc"][:, 3:4],
                         None, ALU.mult, waits=[("dve", visx)] + self.W("utm"))
            self.setW("gout", "dve", vq)
            if _DEV:
                vtx = self.dma("gpsimd", self.dr["dbg_xq"].ap()[c], C["utm"][:],
                               [("dve", vq)])
                self.setW("utm", "sp", vtx)
            vt = 0
            for k in range(KD):
                vt = self.tr(self.ptr[:, k * 128:(k + 1) * 128],
                             C["utm"][:, k * 128:(k + 1) * 128],
                             [("dve", vq)] + self.W("ptr"))
            self.setW("utm", "pe", vt)
            vxq = self.cp("vector", self.xq_v[:, :, c * Q:(c + 1) * Q],
                          self.ptr[:].rearrange("p (k c) -> p k c", k=KD),
                          [("pe", vt)] + ([("pe", vpe_mamba)] if c == 0 else []))
            self.setW("ptr", "dve", vxq)
        vamz = self.mset("vector", fpart[:, 0:4096], 0.0,
                         [("pe", vpe_mamba)])
        vgu = 0
        vpe2 = 0
        for g in range(32):
            bb = g % 2
            vldw = self.dma("sync", self.fwt5[:, bb, 0, :, :],
                            self.dr["wgq"].ap()[:, :, g * 128:(g + 1) * 128].rearrange(
                                "k p c -> p k c"),
                            self.W(f"fwt{bb}") + [("sp", vwq)])
            vldw = self.dma("sync", self.fwt5[:, bb, 1, :, :],
                            self.dr["wuq"].ap()[:, :, g * 128:(g + 1) * 128].rearrange(
                                "k p c -> p k c"))
            for t_ in range(NSLAB):
                for k in range(KD):
                    vpe = self.mm(self.pZ[:, 0:512], self.fwt5[:, bb, 0, k, :],
                                  self.xq_v[:, k, t_ * SLAB:(t_ + 1) * SLAB],
                                  k == 0, k == KD - 1,
                                  ([("ld", vldw), ("dve", vxq)] + self.W("pZg"))
                                  if k == 0 else [])
                for k in range(KD):
                    vpe2 = self.mm(self.pZ[:, 512:1024], self.fwt5[:, bb, 1, k, :],
                                   self.xq_v[:, k, t_ * SLAB:(t_ + 1) * SLAB],
                                   k == 0, k == KD - 1)
                vsg = self.emit("scalar", lambda e: e.activation(
                    C["fg"][:, 0, :], self.pZ[:, 0:512], AF.Silu,
                    scale=C["swb"][:, 3:4]),
                    [("pe", vpe), ("dve", vi)] + self.W("fg"), "act")
                vsu = self.act(C["fg"][:, 1, :], self.pZ[:, 512:1024], AF.Identity,
                               waits=[("pe", vpe2)])
                self.setW("pZg", "act", vsu)
                vgu = self.tt("vector", C["fg"][:, 0, :], C["fg"][:, 0, :],
                              C["fg"][:, 1, :], ALU.mult, [("act", vsu)])
                vsp = self.dma("gpsimd",
                               self.dr["gu_raw"].ap()[g][:, t_ * SLAB:(t_ + 1) * SLAB],
                               C["fg"][:, 0, :], [("dve", vgu)])
                vab = self.act(C["fgu"][:], C["fg"][:, 0, :], AF.Abs,
                               waits=[("dve", vgu)] + self.W("fgu"))
                vgu = self.tt("vector", fpart[:, t_ * SLAB:(t_ + 1) * SLAB],
                              fpart[:, t_ * SLAB:(t_ + 1) * SLAB], C["fgu"][:],
                              ALU.max, [("dve", vamz), ("act", vab)])
                self.setW("fgu", "dve", vgu)
                self.setW("fg", "sp", vsp)
            self.setW(f"fwt{bb}", "pe", vpe2)
        vgusp = self.val("sp")
        v1 = self.cp("vector", fscv[0:64, 0:4096], fpart[64:128, 0:4096],
                     [("dve", vgu)])
        v1 = self.tt("vector", fpart[0:64, 0:4096], fpart[0:64, 0:4096],
                     fscv[0:64, 0:4096], ALU.max)
        v1 = self.cp("vector", fscv[0:32, 0:4096], fpart[32:64, 0:4096])
        v1 = self.tt("vector", fpart[0:32, 0:4096], fpart[0:32, 0:4096],
                     fscv[0:32, 0:4096], ALU.max)
        vfs = 0
        for c in range(NCH):
            vt = self.tr(self.ptr[:, 0:32], fpart[0:32, c * Q:(c + 1) * Q],
                         [("dve", v1)] + self.W("ptr"))
            vv = self.cp("vector", C["utm"][:, 0:32], self.ptr[:, 0:32],
                         [("pe", vt)] + self.W("utm"))
            self.setW("ptr", "dve", vv)
            vv = self.red(C["psc"][:, 0:1], C["utm"][:, 0:32], ALU.max, False,
                          [("dve", vv)])
            self.setW("utm", "dve", vv)
            vv = self.ts("vector", C["psc"][:, 0:1], C["psc"][:, 0:1], 1e-5, None,
                         ALU.max)
            vrc = self.emit("vector", lambda e: e.reciprocal(C["psc"][:, 1:2],
                            C["psc"][:, 0:1]), [("dve", vv)], "dve")
            vfs = self.ts("vector", C["fscale"][:, c, 0:1], C["psc"][:, 1:2], 127.0,
                          None, ALU.mult)
            vfs = self.cp("vector", C["fscale"][:, c, 1:2], C["psc"][:, 0:1])
        vwd = 0
        for k in range(32):
            bb = k % 2
            extra = [("pe", vpe2)] if k < 2 else []
            vldw = self.dma("sync",
                            self.fwt5[:, bb, 0, :, :].rearrange("p k c -> p (k c)"),
                            self.dr["wdq"].ap()[k],
                            self.W(f"fwt{bb}") + extra)
            vwd = self.cp("vector", self.wd_v[:, k, :],
                          self.fwt5[:, bb, 0, :, :].rearrange("p k c -> p (k c)"),
                          [("ld", vldw)] + ([("pe", vpe2)] if k == 0 else []))
            self.setW(f"fwt{bb}", "dve", vwd)
        self.setW("pA2", "dve", self.val("dve"))
        self.setW("fsb", "pe", self.val("pe"))
        self.w["fsb2"] = ("dve", self.val("dve"))
        for c in range(NCH):
            bb = c % 2
            vldg = self.dma("sync", fsbv[:, :, :],
                            self.dr["gu_raw"].ap()[:, :, c * Q:(c + 1) * Q].rearrange(
                                "g p t -> p g t"),
                            self.W("fsb") + self.W("fsb2") + [("sp", vgusp)])
            vt = self.tr(self.pA[0:1, 128:256], C["fscale"][:, c, 0:1],
                         self.W("pA2"))
            vrow = self.cp("vector", C["xnf"][0:1, 0:128], self.pA[0:1, 128:256],
                           [("pe", vt)] + self.W("xnf"))
            vkb = self.mm(self.pA[:, 384:512], C["onesrow"][:],
                          C["xnf"][0:1, 0:128], True, True, [("dve", vrow)])
            vsb = self.cp("vector", C["sclb"][:], self.pA[:, 384:512],
                          [("pe", vkb)])
            self.setW("pA2", "dve", vsb)
            self.setW("xnf", "pe", vkb)
            vq = 0
            for hh in range(4):
                sl = slice(hh * 1024, (hh + 1) * 1024)
                vq = self.ttb("vector", fscv[:, 0:1024],
                              fsbv[:, :, :].rearrange("p g t -> p (g t)")[:, sl],
                              bc2(C["sclb"][:], 8, 128), ALU.mult,
                              [("ld", vldg), ("dve", vsb)] + self.W("fscq"))
                vq = self.cp("vector", C["fsi"][:, 0:1024], fscv[:, 0:1024])
                vq = self.cp("vector",
                             fsbv[:, :, :].rearrange("p g t -> p (g t)")[:, sl],
                             C["fsi"][:, 0:1024])
                self.setW("fscq", "dve", vq)
            for half in range(2):
                for g in range(32):
                    vpe = self.mm(self.pZ[:, half * 512:(half + 1) * 512],
                                  fsbv[:, g, :],
                                  self.wd_v[:, g, half * 512:(half + 1) * 512],
                                  g == 0, g == 31,
                                  ([("dve", vq), ("dve", vwd)] + self.W("pZ"))
                                  if (g == 0 and half == 0) else [])
            self.setW("fsb", "pe", vpe)
            vinv = self.tt("vector", C["psc"][:, 3:4], C["fscale"][:, c, 1:2],
                           C["psc2"][:, 2:3], ALU.mult, [("dve", vswc), ("dve", vfs)])
            vgo = self.emit("scalar", lambda e: e.activation(
                C["gout"][:], self.pZ[:, 0:1024], AF.Silu,
                scale=C["psc"][:, 3:4]), [("pe", vpe), ("dve", vinv)] +
                self.W("gout"), "act")
            self.setW("pZ", "act", vgo)
            vldx = self.dma("sync", C["hres"][:, bb, :],
                            self.dr["x"].ap()[3 + c * Q:3 + (c + 1) * Q, :],
                            self.W(f"hres{bb}"))
            vlds = self.dma("sync", C["h2t"][:],
                            self.dr["h2"].ap()[c * Q:(c + 1) * Q, :],
                            self.W("h2t") + [("sp", self.vh_done)])
            vcb = self.tt("vector", C["gout"][:], C["gout"][:], C["h2t"][:],
                          ALU.mult, [("act", vgo), ("ld", vlds)])
            self.setW("h2t", "dve", vcb)
            vcb = self.tt("vector", C["gout"][:], C["gout"][:], C["hres"][:, bb, :],
                          ALU.add, [("ld", vldx)])
            self.setW(f"hres{bb}", "dve", vcb)
            self.setW("gout", "dve", vcb)
            vout = self.dma("gpsimd", self.dr["out"].ap()[c * Q:(c + 1) * Q, :],
                            C["gout"][:], [("dve", vcb)])
        self.emit("gpsimd", lambda e: e.nop(), [("sp", self.val("sp"))], None)

    def build(self):
        self.declare()
        self.phase0()
        for i in range(2):
            if STAGE >= (1 if i == 0 else 4):
                self.layer(i)
        if STAGE >= 5:
            self.ffn()
        else:
            self.emit("gpsimd", lambda e: e.nop(), [("sp", self.val("sp"))], None)
        self.finalize()
        self.es.close()
        return self.nc


def _mk_inputs(inputs, r):
    b, half = r // 2, r % 2
    t0 = half * T
    x = np.asarray(inputs["L0_attn_x"][b], np.float32)
    xs = np.empty((T + 3, D), np.float32)
    xs[0:3] = x[t0 - 3:t0] if half == 1 else x[0:3]
    xs[3:] = x[t0:t0 + T]
    ipw = np.asarray(inputs["L0_attn_in_proj_w"])
    m = {"x": xs}
    for i in range(2):
        wt = ipw[i].T
        m[f"wtx{i}"] = np.ascontiguousarray(
            wt[:, DI:DI + XD].reshape(KD, 128, XD))
        m[f"wtz{i}"] = np.ascontiguousarray(wt[:, 0:DI].reshape(KD, 128, DI))
        m[f"wo{i}"] = np.ascontiguousarray(
            np.asarray(inputs["L0_attn_out_proj_w"][i]).T.reshape(16, 128, D))
        cw = np.asarray(inputs["L0_attn_conv_w"][i])
        m[f"cw{i}"] = np.ascontiguousarray(
            cw.reshape(NFT, 128, 4).transpose(1, 0, 2).reshape(128, NFT * 4))
        m[f"cb{i}"] = np.ascontiguousarray(
            np.asarray(inputs["L0_attn_conv_b"][i]).reshape(NFT, 128).T)
        m[f"lnwb{i}"] = np.broadcast_to(
            np.asarray(inputs["L0_attn_ln_w"][i]), (128, D)).copy()
        m[f"lnbb{i}"] = np.broadcast_to(
            np.asarray(inputs["L0_attn_ln_b"][i]), (128, D)).copy()
        m[f"dtbb{i}"] = np.broadcast_to(
            np.asarray(inputs["L0_attn_dt_bias"][i]), (128, NDT)).copy()
        m[f"alogb{i}"] = np.broadcast_to(
            np.asarray(inputs["L0_attn_A_log"][i]), (128, NDT)).copy()
        m[f"dbb{i}"] = np.broadcast_to(
            np.asarray(inputs["L0_attn_D"][i]), (128, NDT)).copy()
        m[f"rmsw{i}"] = np.ascontiguousarray(
            np.asarray(inputs["L0_attn_rms_w"][i]).reshape(16, 128).T)
    m["wgt"] = np.ascontiguousarray(
        np.asarray(inputs["L0_ffn_w_gate"]).T.reshape(KD, 128, 4096))
    m["wut"] = np.ascontiguousarray(
        np.asarray(inputs["L0_ffn_w_up"]).T.reshape(KD, 128, 4096))
    m["wdt"] = np.ascontiguousarray(
        np.asarray(inputs["L0_ffn_w_down"]).T.reshape(32, 128, D))
    m["wg_sl"] = np.ascontiguousarray(m["wgt"].reshape(KD * 128, 4096)[
        r * 128:(r + 1) * 128])
    m["wu_sl"] = np.ascontiguousarray(m["wut"].reshape(KD * 128, 4096)[
        r * 128:(r + 1) * 128])
    wdf = m["wdt"].reshape(32 * 128, D)
    m["wd_sl"] = np.ascontiguousarray(
        wdf[r * 512:(r + 1) * 512].reshape(128, 4096))
    m["identb"] = np.eye(128, dtype=ml_dtypes.bfloat16)
    m["triub"] = np.triu(np.ones((128, 128), ml_dtypes.bfloat16))
    m["onesb"] = np.ones((128, 128), ml_dtypes.bfloat16)
    m["triuf"] = np.triu(np.ones((128, 128), np.float32))
    m["onescol"] = np.ones((128, 1), np.float32)
    m["onesrow"] = np.ones((1, 128), np.float32)
    m["identf"] = np.eye(128, dtype=np.float32)
    m["emask"] = np.full((3, 1), 1.0 if half == 1 else 0.0, np.float32)
    pm = np.zeros((64, 8), np.float32)
    if half == 1:
        pm[:, r - 1] = 1.0
    m["pmask_s"] = pm
    pe_ = np.zeros((3, 8), np.float32)
    if half == 1:
        pe_[:, r - 1] = 1.0
    m["pmask_e"] = pe_
    return m


def kernel(**inputs):
    nc = Build().build()
    in_maps = [_mk_inputs(inputs, r) for r in range(8)]
    res = run_bass_kernel_spmd(nc, in_maps, core_ids=list(range(8)))
    out = np.empty((4, 2 * T, D), np.float32)
    for r in range(8):
        b, half = r // 2, r % 2
        out[b, half * T:(half + 1) * T] = res.results[r]["out"]
    return out
